# version 04
# baseline (speedup 1.0000x reference)
import contextlib
import numpy as np
import ml_dtypes
import concourse.bass as bass
import concourse.mybir as mybir
from concourse.bass_utils import run_bass_kernel_spmd

F32 = mybir.dt.float32
BF16 = mybir.dt.bfloat16
I32 = mybir.dt.int32
FP8 = mybir.dt.float8e4
AF = mybir.ActivationFunctionType
ALU = mybir.AluOpType
AX = mybir.AxisListType

HID, NH = 2048, 64
PROJ, FFI, EPS = 8512, 8192, 1e-5
TM, T2 = 1024, 2048
NCORE = 8


class Sem:
    def __init__(self, h, name):
        self.h, self.name, self.n = h, name, 0


class Prog:
    def __init__(self, marks, count_only):
        self.marks = marks
        self.count_only = count_only

    def setup(self, nc, ctx):
        self.SEM = {nm: Sem(ctx.enter_context(nc.semaphore(nm)), nm)
                    for nm in ("s_in", "s_sp", "s_gp", "s_pe", "s_v", "s_a", "s_out")}

    def inc(self, semnm, ins, v=1):
        s = self.SEM[semnm]
        ins.then_inc(s.h, v)
        s.n += v

    def mark(self, name, semnm):
        if self.count_only:
            self.marks[name] = (semnm, self.SEM[semnm].n)

    def incw(self, semnm, eng, ins, v=1):
        sem = self.SEM[semnm]
        ins.then_inc(sem.h, v)
        sem.n += v
        eng.wait_ge(sem.h, sem.n)

    def cmark(self, eng, semnm, name, fence_ap):
        if semnm == "s_v":
            ins = eng.memset(fence_ap, 0.0)
        else:
            ins = eng.memzero(fence_ap)
        self.inc(semnm, ins)
        self.mark(name, semnm)

    def wt(self, eng, name):
        if name not in self.marks:
            assert self.count_only, f"unknown milestone {name}"
            return
        snm, val = self.marks[name]
        if val > 0:
            eng.wait_ge(self.SEM[snm].h, val)


def _mk(build_fn, stage_args=()):
    marks = {}
    build_fn(marks, True, *stage_args)
    return build_fn(marks, False, *stage_args)


def emit_qact_m3(P, v, bufs, src, dst, W):
    nb = W // 32
    q_bm, q_sb, q_ibs, q_bsf, q_bs, q_y, q_8 = (bufs[k] for k in
        ("q_bm", "q_sb", "q_ibs", "q_bsf", "q_bs", "q_y", "q_8"))
    P.incw("s_v", v, v.tensor_reduce(out=q_bm[:, 0:nb], in_=src.rearrange("p (g c) -> p g c", c=32),
                                 axis=AX.X, op=ALU.max, apply_absolute_value=True))
    P.incw("s_v", v, v.tensor_scalar(out=q_bm[:, 0:nb], in0=q_bm[:, 0:nb], scalar1=float(2.0 ** -120),
                                 scalar2=None, op0=ALU.max))
    P.incw("s_v", v, v.tensor_scalar(out=q_sb[:, 0:nb].bitcast(I32), in0=q_bm[:, 0:nb].bitcast(I32),
                                 scalar1=0x7F800000, scalar2=None, op0=ALU.bitwise_and))
    P.incw("s_v", v, v.tensor_scalar(out=q_ibs[:, 0:nb].bitcast(I32), in0=q_sb[:, 0:nb].bitcast(I32),
                                 scalar1=-1, scalar2=-((1 << 31) - (6 << 23)), op0=ALU.mult, op1=ALU.subtract))
    P.incw("s_v", v, v.tensor_scalar(out=q_bsf[:, 0:nb].bitcast(I32), in0=q_sb[:, 0:nb].bitcast(I32),
                                 scalar1=(2 << 23), scalar2=None, op0=ALU.subtract))
    P.incw("s_v", v, v.tensor_copy(out=q_bs[:, 0:nb], in_=q_bsf[:, 0:nb]))
    P.incw("s_v", v, v.tensor_tensor(out=q_y[:, 0:W].rearrange("p (g c) -> p g c", c=32),
                                 in0=src.rearrange("p (g c) -> p g c", c=32),
                                 in1=q_ibs[:, 0:nb].unsqueeze(2).broadcast_to((128, nb, 32)),
                                 op=ALU.mult))
    P.incw("s_v", v, v.tensor_scalar(out=q_8[:, 0:W], in0=q_y[:, 0:W], scalar1=7.5 * 2.0 ** -6,
                                 scalar2=-7.5 * 2.0 ** -6, op0=ALU.min, op1=ALU.max))
    P.incw("s_v", v, v.scalar_tensor_tensor(out=dst.rearrange("p (g c) -> p g c", c=32),
                                        in0=q_8[:, 0:W].rearrange("p (g c) -> p g c", c=32),
                                        scalar=64.0,
                                        in1=q_bs[:, 0:nb].unsqueeze(2).broadcast_to((128, nb, 32)),
                                        op0=ALU.mult, op1=ALU.mult))


def emit_qw_m1(P, v, bufs, src, dst, W):
    nb = W // 32
    q_bm, q_sb, q_ibs, q_bsf, q_bs, q_y, q_cs = (bufs[k] for k in
        ("q_bm", "q_sb", "q_ibs", "q_bsf", "q_bs", "q_y", "q_cs"))
    P.incw("s_v", v, v.tensor_reduce(out=q_bm[:, 0:nb], in_=src.rearrange("p (g c) -> p g c", c=32),
                                 axis=AX.X, op=ALU.max, apply_absolute_value=True))
    P.incw("s_v", v, v.tensor_scalar(out=q_bm[:, 0:nb], in0=q_bm[:, 0:nb], scalar1=float(2.0 ** -120),
                                 scalar2=None, op0=ALU.max))
    P.incw("s_v", v, v.tensor_scalar(out=q_sb[:, 0:nb].bitcast(I32), in0=q_bm[:, 0:nb].bitcast(I32),
                                 scalar1=0x7F800000, scalar2=None, op0=ALU.bitwise_and))
    P.incw("s_v", v, v.tensor_scalar(out=q_ibs[:, 0:nb].bitcast(I32), in0=q_sb[:, 0:nb].bitcast(I32),
                                 scalar1=-1, scalar2=-(1 << 31), op0=ALU.mult, op1=ALU.subtract))
    P.incw("s_v", v, v.tensor_scalar(out=q_bsf[:, 0:nb].bitcast(I32), in0=q_sb[:, 0:nb].bitcast(I32),
                                 scalar1=(2 << 23), scalar2=None, op0=ALU.subtract))
    P.incw("s_v", v, v.tensor_copy(out=q_bs[:, 0:nb], in_=q_bsf[:, 0:nb]))
    P.incw("s_v", v, v.tensor_tensor(out=q_y[:, 0:W].rearrange("p (g c) -> p g c", c=32),
                                 in0=src.rearrange("p (g c) -> p g c", c=32),
                                 in1=q_ibs[:, 0:nb].unsqueeze(2).broadcast_to((128, nb, 32)),
                                 op=ALU.mult))
    P.incw("s_v", v, v.tensor_scalar(out=q_cs[:, 0:W].bitcast(I32), in0=q_y[:, 0:W].bitcast(I32),
                                 scalar1=0x7F800000, scalar2=None, op0=ALU.bitwise_and))
    P.incw("s_v", v, v.tensor_scalar(out=q_cs[:, 0:W].bitcast(I32), in0=q_cs[:, 0:W].bitcast(I32),
                                 scalar1=0x3F800000, scalar2=(22 << 23) + 0x400000,
                                 op0=ALU.max, op1=ALU.add))
    P.incw("s_v", v, v.tensor_tensor(out=q_y[:, 0:W], in0=q_y[:, 0:W], in1=q_cs[:, 0:W], op=ALU.add))
    P.incw("s_v", v, v.tensor_tensor(out=q_y[:, 0:W], in0=q_y[:, 0:W], in1=q_cs[:, 0:W], op=ALU.subtract))
    P.incw("s_v", v, v.tensor_scalar(out=q_y[:, 0:W], in0=q_y[:, 0:W], scalar1=6.0, scalar2=-6.0,
                                 op0=ALU.min, op1=ALU.max))
    P.incw("s_v", v, v.tensor_tensor(out=dst.rearrange("p (g c) -> p g c", c=32),
                                 in0=q_y[:, 0:W].rearrange("p (g c) -> p g c", c=32),
                                 in1=q_bs[:, 0:nb].unsqueeze(2).broadcast_to((128, nb, 32)),
                                 op=ALU.mult))


def qbufs(sbuf, W):
    return {
        "q_bm": sbuf("q_bm", [128, W // 32])[:],
        "q_sb": sbuf("q_sb", [128, W // 32])[:],
        "q_ibs": sbuf("q_ibs", [128, W // 32])[:],
        "q_bsf": sbuf("q_bsf", [128, W // 32])[:],
        "q_bs": sbuf("q_bs", [128, W // 32], BF16)[:],
        "q_y": sbuf("q_y", [128, W])[:],
        "q_cs": sbuf("q_cs", [128, W])[:],
        "q_8": sbuf("q_8", [128, W], FP8)[:],
    }


def build_L1(marks, count_only):
    nc = bass.Bass("TRN2", target_bir_lowering=False)
    xm = nc.dram_tensor("xm", [TM, HID], F32, kind="ExternalInput").ap()
    xr = nc.dram_tensor("xr", [TM, HID], F32, kind="ExternalInput").ap()
    w_inT = nc.dram_tensor("w_inT", [HID, PROJ], F32, kind="ExternalInput").ap()
    convdiag = nc.dram_tensor("convdiag", [136 * 128, 128], F32, kind="ExternalInput").ap()
    convbI = nc.dram_tensor("convb", [128, 34], F32, kind="ExternalInput").ap()
    identbI = nc.dram_tensor("identb", [128, 128], BF16, kind="ExternalInput").ap()

    qxnO = nc.dram_tensor("qxnO", [TM, HID], BF16, kind="ExternalOutput").ap()
    sgO = nc.dram_tensor("sgO", [4096, TM], BF16, kind="ExternalOutput").ap()
    hsO = nc.dram_tensor("hsO", [4096, T2], BF16, kind="ExternalOutput").ap()
    BfO = nc.dram_tensor("BfO", [128, T2], BF16, kind="ExternalOutput").ap()
    CfO = nc.dram_tensor("CfO", [128, T2], BF16, kind="ExternalOutput").ap()
    dtO = nc.dram_tensor("dtO", [NH, T2], F32, kind="ExternalOutput").ap()
    xnO = nc.dram_tensor("xnO", [128, HID], F32, kind="ExternalOutput").ap()
    statsO = nc.dram_tensor("statsO", [128, 12], F32, kind="ExternalOutput").ap()
    xnTO = nc.dram_tensor("xnTO", [128, T2], BF16, kind="ExternalOutput").ap()
    cvO = nc.dram_tensor("cvO", [128, T2 + 3], BF16, kind="ExternalOutput").ap()

    P = Prog(marks, count_only)
    ctx = contextlib.ExitStack()
    with ctx:
        def sbuf(name, shape, dt=F32):
            return ctx.enter_context(nc.sbuf_tensor(name, shape, dt))

        xnT = [sbuf(f"xnT{i}", [128, T2], BF16) for i in range(16)]
        convin = sbuf("convin", [128, T2 + 3], BF16)
        evb = [sbuf(f"evb{i}", [128, T2], BF16) for i in range(2)]
        sgb = sbuf("sgb", [128, TM], BF16)
        dtb_ = sbuf("dtb_", [NH, T2])
        xt2 = [sbuf(f"xt2{i}", [128, HID]) for i in range(2)]
        xn32 = sbuf("xn32", [128, HID])
        sqscr = sbuf("sqscr", [128, HID])
        xnb = sbuf("xnb", [128, HID], BF16)
        qxn = sbuf("qxn", [128, HID], BF16)
        stats = sbuf("stats", [128, 12])
        junk = sbuf("junk", [128, 8])
        wbuf = [sbuf(f"wbuf{i}", [128, HID]) for i in range(2)]
        wcast = [sbuf(f"wcast{i}", [128, HID], BF16) for i in range(2)]
        cdgf = sbuf("cdgf", [128, 512])
        cdg = [sbuf(f"cdg{i}", [128, 512], BF16) for i in range(2)]
        convbS = sbuf("convbS", [128, 34])
        idb = sbuf("idb", [128, 128], BF16)
        QB = qbufs(sbuf, HID)
        pA = [ctx.enter_context(nc.psum_tensor(f"pA{i}", [128, 512], F32)) for i in range(4)]
        pB = [ctx.enter_context(nc.psum_tensor(f"pB{i}", [128, 512], F32)) for i in range(4)]
        P.setup(nc, ctx)
        blk = ctx.enter_context(nc.Block())

        @blk.sync
        def _(sync):
            P.inc("s_in", sync.dma_start(out=convbS[:], in_=convbI), 16)
            P.inc("s_in", sync.dma_start(out=idb[:], in_=identbI), 16)
            P.mark("smalls", "s_in")
            for i in range(16):
                src = xr if i < 8 else xm
                r0 = (i % 8) * 128
                if i >= 2:
                    P.wt(sync, f"xnmade{i-2}")
                P.inc("s_in", sync.dma_start(out=xt2[i % 2][:], in_=src[r0:r0 + 128, :]), 16)
                P.mark(f"x{i}", "s_in")
            for j in range(67):
                wj = 128 if j < 66 else 64
                if j >= 2:
                    P.wt(sync, f"wcast{j-2}")
                P.inc("s_in", sync.dma_start(
                    out=wbuf[j % 2][:, 0:16 * wj],
                    in_=w_inT.rearrange("(a p) c -> p a c", p=128)[:, :, j * 128:j * 128 + wj]), 16)
                P.mark(f"wload{j}", "s_in")
                if 32 <= j < 66:
                    cc = j - 32
                    if cc >= 1:
                        P.wt(sync, f"cdgcast{cc-1}")
                    P.inc("s_in", sync.dma_start(
                        out=cdgf[:].rearrange("p (t j) -> p t j", j=128),
                        in_=convdiag.rearrange("(t p) j -> p t j", p=128)[:, cc:136:34, :]), 16)
                    P.mark(f"cdg{cc}", "s_in")

        @blk.scalar
        def _(sc):
            for i in range(16):
                P.wt(sc, f"x{i}")
                if i >= 1:
                    P.wt(sc, f"xnuse{i-1}")
                P.incw("s_a", sc, sc.activation(out=sqscr[:], in_=xt2[i % 2][:], func=AF.Square))
                P.cmark(sc, "s_a", f"sq{i}", junk[0:1, 0:1])
                P.wt(sc, f"var{i}")
                P.incw("s_a", sc, sc.activation(out=stats[:, 8:9], in_=stats[:, 5:6], func=AF.Ln))
                P.incw("s_a", sc, sc.activation(out=stats[:, 6:7], in_=stats[:, 8:9], func=AF.Exp, scale=-0.5))
                P.cmark(sc, "s_a", f"rstd{i}", junk[0:1, 0:1])
                P.wt(sc, f"bias{i}")
                P.incw("s_a", sc, sc.activation(out=xn32[:], in_=xt2[i % 2][:], func=AF.Identity,
                                           bias=stats[:, 7:8], scale=stats[:, 6:7]))
                P.cmark(sc, "s_a", f"xnmade{i}", junk[0:1, 0:1])
                P.wt(sc, f"tr{i}")
                for k in range(16):
                    P.incw("s_a", sc, sc.copy(out=xnT[k][:, i * 128:(i + 1) * 128],
                                         in_=pA[k // 8][:].bitcast(BF16)[:, (k % 8) * 128:(k % 8 + 1) * 128]))
                P.cmark(sc, "s_a", f"trev{i}", junk[0:1, 0:1])
            for j in range(67):
                P.wt(sc, f"mm{j}")
                if j < 32:
                    if j >= 1:
                        P.wt(sc, f"sgsp{j-1}")
                    for h in (0, 1):
                        P.incw("s_a", sc, sc.activation(out=sgb[:, h * 512:(h + 1) * 512],
                                                   in_=pA[2 + h][:], func=AF.Silu))
                    P.incw("s_a", sc, sc.memzero(junk[0:1, 2:3]))
                    P.cmark(sc, "s_a", f"sgf{j}", junk[0:1, 0:1])
                elif j < 66:
                    if j >= 33:
                        P.wt(sc, f"conv{j-33}")
                    if j == 38:
                        P.wt(sc, "cvtapd")
                    for h in range(4):
                        P.incw("s_a", sc, sc.copy(out=convin[:, 3 + h * 512:3 + (h + 1) * 512], in_=pA[h][:]))
                else:
                    for h in range(4):
                        P.incw("s_a", sc, sc.copy(out=dtb_[:, h * 512:(h + 1) * 512], in_=pA[h][0:64, :]))
                    P.incw("s_a", sc, sc.memzero(junk[0:1, 4:5]))
                    P.cmark(sc, "s_a", "dtf", junk[0:1, 0:1])
                P.cmark(sc, "s_a", f"evacA{j}", junk[0:1, 0:1])
                if j == 37:
                    P.cmark(sc, "s_a", "cvtap", junk[0:1, 0:1])
                if 33 <= j <= 66:
                    cc = j - 33
                    P.wt(sc, f"conv{cc}")
                    if cc >= 2:
                        P.wt(sc, f"hssp{cc-2}")
                    for h in range(4):
                        P.incw("s_a", sc, sc.activation(out=evb[cc % 2][:, h * 512:(h + 1) * 512], in_=pB[h][:],
                                                   func=AF.Silu, bias=convbS[:, cc:cc + 1]))
                    P.incw("s_a", sc, sc.memzero(junk[0:1, 3:4]))
                    P.cmark(sc, "s_a", f"convf{cc}", junk[0:1, 0:1])

        @blk.tensor
        def _(te):
            P.wt(te, "smalls")
            for i in range(16):
                P.wt(te, f"xnb{i}")
                if i >= 1:
                    P.wt(te, f"trev{i-1}")
                ins = None
                for k in range(16):
                    ins = te.matmul(pA[k // 8][:].bitcast(BF16)[:, (k % 8) * 128:(k % 8 + 1) * 128],
                                    xnb[:, k * 128:(k + 1) * 128], idb[:],
                                    is_transpose=True, start=(k % 8 == 0), stop=(k % 8 == 7))
                P.inc("s_pe", ins)
                P.mark(f"tr{i}", "s_pe")
            for j in range(67):
                wj = 128 if j < 66 else 64
                P.wt(te, f"wcast{j}")
                if j == 0:
                    P.wt(te, "trev15")
                if j >= 1:
                    P.wt(te, f"evacA{j-1}")
                ins = None
                for h in ((2, 3) if j < 32 else (0, 1, 2, 3)):
                    for k in range(16):
                        dstp = pA[h][:] if wj == 128 else pA[h][0:64, :]
                        ins = te.matmul(dstp, wcast[j % 2][:, k * wj:(k + 1) * wj],
                                        xnT[k][:, h * 512:(h + 1) * 512],
                                        start=(k == 0), stop=(k == 15))
                P.inc("s_pe", ins)
                P.mark(f"mm{j}", "s_pe")
                if 33 <= j <= 66:
                    cc = j - 33
                    P.wt(te, f"cdgcast{cc}")
                    if cc >= 1:
                        P.wt(te, f"convf{cc-1}")
                    ins = None
                    for h in range(4):
                        for k in range(4):
                            ins = te.matmul(pB[h][:], cdg[cc % 2][:, k * 128:(k + 1) * 128],
                                            convin[:, k + h * 512:k + h * 512 + 512],
                                            start=(k == 0), stop=(k == 3))
                    P.inc("s_pe", ins)
                    P.mark(f"conv{cc}", "s_pe")

        @blk.vector
        def _(v):
            P.incw("s_v", v, v.memset(convin[:, 0:3], 0.0))
            for i in range(16):
                P.wt(v, f"sq{i}")
                st = stats
                P.incw("s_v", v, v.tensor_reduce(out=st[:, 0:1], in_=xt2[i % 2][:], axis=AX.X, op=ALU.add))
                P.incw("s_v", v, v.tensor_reduce(out=st[:, 1:2], in_=sqscr[:], axis=AX.X, op=ALU.add))
                P.incw("s_v", v, v.tensor_scalar(out=st[:, 2:3], in0=st[:, 0:1], scalar1=1.0 / HID, scalar2=None, op0=ALU.mult))
                P.incw("s_v", v, v.tensor_scalar(out=st[:, 3:4], in0=st[:, 1:2], scalar1=1.0 / HID, scalar2=None, op0=ALU.mult))
                P.incw("s_v", v, v.tensor_tensor(out=st[:, 4:5], in0=st[:, 2:3], in1=st[:, 2:3], op=ALU.mult))
                P.incw("s_v", v, v.tensor_tensor(out=st[:, 5:6], in0=st[:, 3:4], in1=st[:, 4:5], op=ALU.subtract))
                P.incw("s_v", v, v.tensor_scalar(out=st[:, 5:6], in0=st[:, 5:6], scalar1=EPS, scalar2=None, op0=ALU.add))
                P.cmark(v, "s_v", f"var{i}", junk[:, 0:1])
                P.wt(v, f"rstd{i}")
                P.incw("s_v", v, v.tensor_tensor(out=st[:, 7:8], in0=st[:, 2:3], in1=st[:, 6:7], op=ALU.mult))
                P.incw("s_v", v, v.tensor_scalar(out=st[:, 7:8], in0=st[:, 7:8], scalar1=-1.0, scalar2=None, op0=ALU.mult))
                P.cmark(v, "s_v", f"bias{i}", junk[:, 0:1])
                P.wt(v, f"xnmade{i}")
                if i >= 1:
                    P.wt(v, f"tr{i-1}")
                P.incw("s_v", v, v.tensor_copy(out=xnb[:], in_=xn32[:]))
                P.cmark(v, "s_v", f"xnb{i}", junk[:, 0:1])
                if i >= 8:
                    if i >= 9:
                        P.wt(v, f"qxnsp{i-1}")
                    emit_qact_m3(P, v, QB, xn32[:], qxn[:], HID)
                    P.incw("s_v", v, v.memset(junk[:, 0:1], 0.0))
                    P.cmark(v, "s_v", f"qxnf{i}", junk[:, 0:1])
                P.cmark(v, "s_v", f"xnuse{i}", junk[:, 0:1])
            for j in range(67):
                wj = 128 if j < 66 else 64
                P.wt(v, f"wload{j}")
                if j >= 2:
                    P.wt(v, f"mm{j-2}")
                P.incw("s_v", v, v.tensor_copy(out=wcast[j % 2][:, 0:16 * wj], in_=wbuf[j % 2][:, 0:16 * wj]))
                P.cmark(v, "s_v", f"wcast{j}", junk[:, 0:1])
                if 32 <= j < 66:
                    cc = j - 32
                    P.wt(v, f"cdg{cc}")
                    if cc >= 2:
                        P.wt(v, f"conv{cc-2}")
                    P.incw("s_v", v, v.tensor_copy(out=cdg[cc % 2][:], in_=cdgf[:]))
                    P.cmark(v, "s_v", f"cdgcast{cc}", junk[:, 0:1])

        @blk.gpsimd
        def _(g):
            for i in range(8, 16):
                P.wt(g, f"qxnf{i}")
                P.inc("s_gp", g.dma_start(out=qxnO[(i - 8) * 128:(i - 7) * 128, :], in_=qxn[:]), 16)
                P.mark(f"qxnsp{i}", "s_gp")
            for j in range(32):
                P.wt(g, f"sgf{j}")
                P.inc("s_gp", g.dma_start(out=sgO[j * 128:(j + 1) * 128, :], in_=sgb[:]), 16)
                P.mark(f"sgsp{j}", "s_gp")
            for cc in range(34):
                P.wt(g, f"convf{cc}")
                dst = hsO[cc * 128:(cc + 1) * 128, :] if cc < 32 else (BfO if cc == 32 else CfO)
                P.inc("s_gp", g.dma_start(out=dst, in_=evb[cc % 2][:]), 16)
                P.mark(f"hssp{cc}", "s_gp")
                if cc == 4:
                    P.wt(g, "cvtap")
                    P.inc("s_out", g.dma_start(out=cvO, in_=convin[:]), 16)
                    P.mark("cvtapd", "s_out")
            P.wt(g, "dtf")
            P.inc("s_out", g.dma_start(out=dtO, in_=dtb_[:]), 16)
            P.wt(g, "xnuse15")
            P.inc("s_out", g.dma_start(out=xnO, in_=xn32[:]), 16)
            P.inc("s_out", g.dma_start(out=statsO, in_=stats[:]), 16)
            P.wt(g, "trev15")
            P.inc("s_out", g.dma_start(out=xnTO, in_=xnT[3][:]), 16)
            P.mark("done", "s_out")
            g.wait_ge(P.SEM["s_out"].h, P.SEM["s_out"].n)
            g.wait_ge(P.SEM["s_gp"].h, P.SEM["s_gp"].n)
    return nc


def build_L2(marks, count_only):
    nc = bass.Bass("TRN2", target_bir_lowering=False)
    hsI = nc.dram_tensor("hsI", [4096, T2], BF16, kind="ExternalInput").ap()
    BfI = nc.dram_tensor("BfI", [128, T2], BF16, kind="ExternalInput").ap()
    CfI = nc.dram_tensor("CfI", [128, T2], BF16, kind="ExternalInput").ap()
    dtI = nc.dram_tensor("dtI", [NH, T2], F32, kind="ExternalInput").ap()
    sgI = nc.dram_tensor("sgI", [4096, TM], BF16, kind="ExternalInput").ap()
    dtbI = nc.dram_tensor("dtb", [NH, 1], F32, kind="ExternalInput").ap()
    alogI = nc.dram_tensor("alog", [NH, 1], F32, kind="ExternalInput").ap()
    dvecI = nc.dram_tensor("dvec", [1, NH], F32, kind="ExternalInput").ap()
    identbI = nc.dram_tensor("identb", [128, 128], BF16, kind="ExternalInput").ap()
    identfI = nc.dram_tensor("identf", [128, 128], F32, kind="ExternalInput").ap()
    utnegI = nc.dram_tensor("utneg", [128, 128], F32, kind="ExternalInput").ap()
    mvecI = nc.dram_tensor("mvec", [128, 1], F32, kind="ExternalInput").ap()
    ygnO = nc.dram_tensor("ygnO", [4096, TM], BF16, kind="ExternalOutput").ap()
    cavO = nc.dram_tensor("cavO", [NH, T2], F32, kind="ExternalOutput").ap()
    caTO = nc.dram_tensor("caTO", [128, NH], F32, kind="ExternalOutput").ap()
    eEO = nc.dram_tensor("eEO", [128, 128], F32, kind="ExternalOutput").ap()
    mddO = nc.dram_tensor("mddO", [128, 128], BF16, kind="ExternalOutput").ap()
    dtsO = nc.dram_tensor("dtsO", [NH, T2], F32, kind="ExternalOutput").ap()

    P = Prog(marks, count_only)
    ctx = contextlib.ExitStack()
    with ctx:
        def sbuf(name, shape, dt=F32):
            return ctx.enter_context(nc.sbuf_tensor(name, shape, dt))

        Bf = sbuf("Bf", [128, T2], BF16)
        Cf = sbuf("Cf", [128, T2], BF16)
        dts = sbuf("dts", [NH, T2])
        spx = sbuf("spx", [NH, T2])
        cav = sbuf("cav", [NH, T2])
        catot = sbuf("catot", [NH, 16])
        catb = sbuf("catb", [NH, 16], BF16)
        s64a = sbuf("s64a", [NH, 128])
        s64b = sbuf("s64b", [NH, 16])
        s64c = sbuf("s64c", [NH, 128])
        wdtF = sbuf("wdtF", [NH, 128])
        zeros64 = sbuf("zeros64", [NH, 128])
        ones64 = sbuf("ones64", [NH, 1])
        aneg = sbuf("aneg", [NH, 1])
        dtbS = sbuf("dtbS", [NH, 1])
        S = sbuf("S", [128, 4096])
        Sb = sbuf("Sb", [128, 4096], BF16)
        hstok = sbuf("hstok", [128, 4096], BF16)
        Btok = sbuf("Btok", [128, 128], BF16)
        whs = sbuf("whs", [128, 4096], BF16)
        sgsl = sbuf("sgsl", [128, 4096], BF16)
        caT = sbuf("caT", [128, NH])
        ncaT = sbuf("ncaT", [128, NH])
        dtT = sbuf("dtT", [128, NH], BF16)
        ecaT = sbuf("ecaT", [128, NH])
        wdtT = sbuf("wdtT", [128, NH], BF16)
        decB = sbuf("decB", [128, NH])
        dvB = sbuf("dvB", [128, NH], BF16)
        utnB = sbuf("utnB", [128, 128])
        rhs2 = sbuf("rhs2", [128, 128])
        mvS = sbuf("mvS", [128, 1])
        idb = sbuf("idb", [128, 128], BF16)
        idf = sbuf("idf", [128, 128])
        onesb1 = sbuf("onesb1", [1, 128], BF16)
        dv16 = sbuf("dv16", [1, NH], BF16)
        dvSf = sbuf("dvSf", [1, NH])
        expE = sbuf("expE", [128, 128])
        mdd = sbuf("mdd", [128, 128], BF16)
        e2t = sbuf("e2t", [128, 512])
        ysum = sbuf("ysum", [128, 512])
        skt = sbuf("skt", [128, 512])
        ytok = sbuf("ytok", [128, 512], BF16)
        ytgnS = sbuf("ytgnS", [128, 512], BF16)
        junk = sbuf("junk", [128, 8])
        pA = [ctx.enter_context(nc.psum_tensor(f"pA{i}", [128, 512], F32)) for i in range(4)]
        pB = [ctx.enter_context(nc.psum_tensor(f"pB{i}", [128, 512], F32)) for i in range(4)]
        P.setup(nc, ctx)
        blk = ctx.enter_context(nc.Block())
        NG = 8

        @blk.sync
        def _(sync):
            for ap_, src in ((dtbS, dtbI), (aneg, alogI), (dvSf, dvecI), (idb, identbI),
                             (idf, identfI), (utnB, utnegI), (mvS, mvecI),
                             (Bf, BfI), (Cf, CfI), (dts, dtI)):
                P.inc("s_in", sync.dma_start(out=ap_[:], in_=src), 16)
            P.mark("smalls", "s_in")
            for g in range(16):
                if g >= 1:
                    if g - 1 >= 8:
                        P.wt(sync, f"ygnf_{g-1}_7")
                    P.wt(sync, f"supd_{g-1}_3")
                for q in range(4):
                    P.inc("s_in", sync.dma_start_transpose(
                        out=hstok[:, q * 1024:(q + 1) * 1024],
                        in_=hsI[q * 1024:(q + 1) * 1024, g * 128:(g + 1) * 128]), 16)
                P.inc("s_in", sync.dma_start_transpose(
                    out=Btok[:], in_=BfI[:, g * 128:(g + 1) * 128]), 16)
                P.mark(f"hstok{g}", "s_in")
                if g >= 8:
                    if g >= 9:
                        P.wt(sync, f"ygnspall{g-1}")
                    P.inc("s_in", sync.dma_start(
                        out=sgsl[:],
                        in_=sgI.rearrange("(a p) t -> p a t", p=128)[:, :, (g - 8) * 128:(g - 7) * 128]), 16)
                    P.mark(f"sgsl{g}", "s_in")

        @blk.scalar
        def _(sc):
            P.wt(sc, "smalls")
            P.incw("s_a", sc, sc.activation(out=aneg[:], in_=aneg[:], func=AF.Exp))
            P.cmark(sc, "s_a", "anegE", junk[0:1, 0:1])
            P.wt(sc, "absu")
            P.incw("s_a", sc, sc.activation(out=spx[:], in_=spx[:], func=AF.Exp, scale=-1.0))
            P.incw("s_a", sc, sc.activation(out=spx[:], in_=spx[:], func=AF.Ln, bias=ones64[:]))
            P.cmark(sc, "s_a", "lnexp", junk[0:1, 0:1])
            P.wt(sc, "catot")
            P.incw("s_a", sc, sc.activation(out=s64b[:], in_=catot[:], func=AF.Exp))
            P.cmark(sc, "s_a", "decF", junk[0:1, 0:1])
            for g in range(16):
                P.wt(sc, f"wdtarg{g}")
                if g >= 1:
                    P.wt(sc, f"wdtT{g-1}")
                P.incw("s_a", sc, sc.activation(out=wdtF[:], in_=s64a[:], func=AF.Exp))
                P.cmark(sc, "s_a", f"wdtexp{g}", junk[0:1, 0:1])
                if g >= 8:
                    P.wt(sc, f"cafree{g}")
                    P.incw("s_a", sc, sc.activation(out=s64c[:], in_=cav[:, g * 128:(g + 1) * 128], func=AF.Exp))
                    P.cmark(sc, "s_a", f"ecaF{g}", junk[0:1, 0:1])
                    for h in range(NH):
                        P.wt(sc, f"D_{g}_{h}")
                        P.incw("s_a", sc, sc.activation(out=expE[:], in_=pB[0][:, 0:128], func=AF.Exp))
                        P.cmark(sc, "s_a", f"expE_{g}_{h}", junk[0:1, 0:1])

        @blk.tensor
        def _(te):
            P.wt(te, "dv16")
            P.inc("s_pe", te.matmul(pB[2][:, 0:64], onesb1[:], dv16[:], start=True, stop=True))
            P.mark("dvBmm", "s_pe")
            for g in range(16):
                P.wt(te, f"wdtexp2{g}")
                if g >= 1:
                    P.wt(te, f"pb3free{g-1}")
                P.inc("s_pe", te.matmul(pB[3][:, 0:64], wdtF[:], idf[0:64, 0:64],
                                        is_transpose=True, start=True, stop=True))
                P.mark(f"wdtT{g}", "s_pe")
                P.wt(te, "decb")
                P.wt(te, f"wdtTev{g}")
                P.inc("s_pe", te.matmul(pB[3][:, 64:128], catb[:, g:g + 1].broadcast_to((64, 128)),
                                        idb[0:64, 0:64], start=True, stop=True))
                P.mark(f"decmm{g}", "s_pe")
                if g >= 8:
                    P.wt(te, f"decev{g}")
                    P.inc("s_pe", te.matmul(pB[3][:, 0:64], cav[:, g * 128:(g + 1) * 128],
                                            idf[0:64, 0:64], is_transpose=True, start=True, stop=True))
                    P.mark(f"caTmm{g}", "s_pe")
                    P.wt(te, f"caTev{g}")
                    P.inc("s_pe", te.matmul(pB[3][:, 64:128], dts[:, g * 128:(g + 1) * 128],
                                            idf[0:64, 0:64], is_transpose=True, start=True, stop=True))
                    P.mark(f"dtTmm{g}", "s_pe")
                    P.wt(te, f"ecaF{g}")
                    P.wt(te, f"dtTev{g}")
                    P.inc("s_pe", te.matmul(pB[3][:, 0:64], s64c[:], idf[0:64, 0:64],
                                            is_transpose=True, start=True, stop=True))
                    P.mark(f"ecaTmm{g}", "s_pe")
                    if g >= 9:
                        P.wt(te, f"mddlast{g-1}")
                    P.inc("s_pe", te.matmul(pB[1][:, 0:128], Bf[:, g * 128:(g + 1) * 128],
                                            Cf[:, g * 128:(g + 1) * 128], start=True, stop=True))
                    P.mark(f"Gt{g}", "s_pe")
                    for gr in range(NG):
                        P.wt(te, f"Sbready{g}")
                        if gr >= 1:
                            P.wt(te, f"e2used_{g}_{gr-1}")
                        P.inc("s_pe", te.matmul(pB[2][:], Cf[:, g * 128:(g + 1) * 128],
                                                Sb[:, gr * 512:(gr + 1) * 512], start=True, stop=True))
                        P.mark(f"Y2_{g}_{gr}", "s_pe")
                        for hh in range(8):
                            h = gr * 8 + hh
                            P.wt(te, f"rhs2_{g}_{h}")
                            if h >= 1:
                                P.wt(te, f"expE_{g}_{h-1}")
                            te.matmul(pB[0][:, 0:128], caT[:, h:h + 1].broadcast_to((128, 128)),
                                      idf[:], start=True, stop=False)
                            P.inc("s_pe", te.matmul(pB[0][:, 0:128], idf[:], rhs2[:],
                                                    start=False, stop=True))
                            P.mark(f"D_{g}_{h}", "s_pe")
                            P.wt(te, f"mdd_{g}_{h}")
                            if hh == 0 and gr >= 1:
                                P.wt(te, f"ysumused_{g}_{gr-1}")
                            P.inc("s_pe", te.matmul(pA[2][:, hh * 64:(hh + 1) * 64], mdd[:],
                                                    hstok[:, h * 64:(h + 1) * 64],
                                                    start=(hh == 0), stop=(hh == 7)))
                            P.mark(f"Y1_{g}_{h}", "s_pe")
                        P.wt(te, f"ytok_{g}_{gr}")
                        if gr >= 1:
                            P.wt(te, f"ytgnused_{g}_{gr-1}")
                        ins = None
                        for p4 in range(4):
                            ins = te.matmul(pA[3][:].bitcast(BF16)[:, p4 * 128:(p4 + 1) * 128],
                                            ytok[:, p4 * 128:(p4 + 1) * 128], idb[:],
                                            is_transpose=True, start=(p4 == 0), stop=(p4 == 3))
                        P.inc("s_pe", ins)
                        P.mark(f"ytT_{g}_{gr}", "s_pe")
                    P.mark(f"mddlast{g}", "s_pe")
                P.wt(te, f"whs{g}")
                for r in range(4):
                    if r >= 1:
                        P.wt(te, f"supd_{g}_{r-1}")
                    elif g >= 1:
                        P.wt(te, f"supd_{g-1}_3")
                    P.inc("s_pe", te.matmul(pA[0][:], Btok[:], whs[:, r * 1024:r * 1024 + 512],
                                            start=True, stop=True))
                    P.inc("s_pe", te.matmul(pA[1][:], Btok[:], whs[:, r * 1024 + 512:(r + 1) * 1024],
                                            start=True, stop=True))
                    P.mark(f"dS_{g}_{r}", "s_pe")

        @blk.vector
        def _(v):
            P.wt(v, "smalls")
            P.incw("s_v", v, v.memset(zeros64[:], 0.0))
            P.incw("s_v", v, v.memset(ones64[:], 1.0))
            P.incw("s_v", v, v.memset(S[:], 0.0))
            P.incw("s_v", v, v.memset(Sb[:], 0.0))
            P.incw("s_v", v, v.memset(onesb1[:], 1.0))
            P.incw("s_v", v, v.tensor_copy(out=dv16[:], in_=dvSf[:]))
            P.cmark(v, "s_v", "dv16", junk[:, 0:1])
            P.cmark(v, "s_v", "Sbready8", junk[:, 0:1])
            P.wt(v, "dvBmm")
            P.incw("s_v", v, v.tensor_copy(out=dvB[:, 0:64], in_=pB[2][:, 0:64]))
            P.cmark(v, "s_v", "dvBev", junk[:, 0:1])
            P.wt(v, "anegE")
            P.incw("s_v", v, v.tensor_scalar(out=aneg[:], in0=aneg[:], scalar1=-1.0, scalar2=None, op0=ALU.mult))
            P.incw("s_v", v, v.tensor_scalar(out=dts[:], in0=dts[:], scalar1=dtbS[:], scalar2=None, op0=ALU.add))
            P.incw("s_v", v, v.tensor_scalar(out=spx[:].bitcast(I32), in0=dts[:].bitcast(I32),
                                         scalar1=0x7FFFFFFF, scalar2=None, op0=ALU.bitwise_and))
            P.cmark(v, "s_v", "absu", junk[:, 0:1])
            P.wt(v, "lnexp")
            P.incw("s_v", v, v.tensor_scalar(out=dts[:], in0=dts[:], scalar1=0.0, scalar2=None, op0=ALU.max))
            P.incw("s_v", v, v.tensor_tensor(out=dts[:], in0=dts[:], in1=spx[:], op=ALU.add))
            P.incw("s_v", v, v.tensor_scalar(out=spx[:], in0=dts[:], scalar1=aneg[:], scalar2=None, op0=ALU.mult))
            for g in range(16):
                P.incw("s_v", v, v.tensor_tensor_scan(out=cav[:, g * 128:(g + 1) * 128],
                                                  data0=spx[:, g * 128:(g + 1) * 128],
                                                  data1=zeros64[:], initial=0.0, op0=ALU.add, op1=ALU.add))
            P.incw("s_v", v, v.tensor_copy(out=catot[:], in_=cav[:, 127::128]))
            P.cmark(v, "s_v", "catot", junk[:, 0:1])
            P.wt(v, "decF")
            P.incw("s_v", v, v.tensor_copy(out=catb[:], in_=s64b[:]))
            P.cmark(v, "s_v", "decb", junk[:, 0:1])
            for g in range(16):
                if g >= 1:
                    P.wt(v, f"wdtexp{g-1}")
                P.incw("s_v", v, v.tensor_scalar(out=s64a[:], in0=cav[:, g * 128:(g + 1) * 128],
                                             scalar1=-1.0, scalar2=catot[:, g:g + 1], op0=ALU.mult, op1=ALU.add))
                P.cmark(v, "s_v", f"wdtarg{g}", junk[:, 0:1])
                P.wt(v, f"wdtexp{g}")
                P.incw("s_v", v, v.tensor_tensor(out=wdtF[:], in0=wdtF[:],
                                             in1=dts[:, g * 128:(g + 1) * 128], op=ALU.mult))
                P.cmark(v, "s_v", f"wdtexp2{g}", junk[:, 0:1])
                P.wt(v, f"wdtT{g}")
                P.incw("s_v", v, v.tensor_copy(out=wdtT[:, 0:64], in_=pB[3][:, 0:64]))
                P.cmark(v, "s_v", f"wdtTev{g}", junk[:, 0:1])
                P.wt(v, f"decmm{g}")
                P.incw("s_v", v, v.tensor_copy(out=decB[:, 0:64], in_=pB[3][:, 64:128]))
                P.cmark(v, "s_v", f"decev{g}", junk[:, 0:1])
                if g < 8:
                    P.cmark(v, "s_v", f"pb3free{g}", junk[:, 0:1])
                P.wt(v, f"hstok{g}")
                P.incw("s_v", v, v.tensor_tensor(out=whs[:].rearrange("p (h d) -> p h d", d=64),
                                             in0=hstok[:].rearrange("p (h d) -> p h d", d=64),
                                             in1=wdtT[:, 0:64].unsqueeze(2).broadcast_to((128, 64, 64)),
                                             op=ALU.mult))
                P.cmark(v, "s_v", f"whs{g}", junk[:, 0:1])
                if g >= 8:
                    P.wt(v, f"caTmm{g}")
                    if g == 10:
                        P.wt(v, "captapd")
                    P.incw("s_v", v, v.tensor_copy(out=caT[:, 0:64], in_=pB[3][:, 0:64]))
                    if g == 9:
                        P.cmark(v, "s_v", "captap", junk[:, 0:1])
                    P.incw("s_v", v, v.tensor_scalar(out=ncaT[:, 0:64], in0=caT[:, 0:64], scalar1=-1.0,
                                                 scalar2=None, op0=ALU.mult))
                    P.cmark(v, "s_v", f"caTev{g}", junk[:, 0:1])
                    P.cmark(v, "s_v", f"cafree{g}", junk[:, 0:1])
                    P.wt(v, f"dtTmm{g}")
                    P.incw("s_v", v, v.tensor_copy(out=dtT[:, 0:64], in_=pB[3][:, 64:128]))
                    P.cmark(v, "s_v", f"dtTev{g}", junk[:, 0:1])
                    P.wt(v, f"ecaTmm{g}")
                    P.incw("s_v", v, v.tensor_copy(out=ecaT[:, 0:64], in_=pB[3][:, 0:64]))
                    P.cmark(v, "s_v", f"ecaTev{g}", junk[:, 0:1])
                    P.cmark(v, "s_v", f"pb3free{g}", junk[:, 0:1])
                    P.wt(v, f"sgsl{g}")
                    for gr in range(NG):
                        for hh in range(8):
                            h = gr * 8 + hh
                            if h >= 1:
                                P.wt(v, f"D_{g}_{h-1}")
                            P.incw("s_v", v, v.tensor_scalar(out=rhs2[:], in0=utnB[:], scalar1=ncaT[:, h:h + 1],
                                                         scalar2=None, op0=ALU.add))
                            P.cmark(v, "s_v", f"rhs2_{g}_{h}", junk[:, 0:1])
                            P.wt(v, f"expE_{g}_{h}")
                            P.wt(v, f"Gt{g}")
                            if h >= 1:
                                P.wt(v, f"Y1_{g}_{h-1}")
                            P.incw("s_v", v, v.scalar_tensor_tensor(out=mdd[:], in0=expE[:],
                                                                scalar=dtT[:, h:h + 1], in1=pB[1][:, 0:128],
                                                                op0=ALU.mult, op1=ALU.mult))
                            P.cmark(v, "s_v", f"mdd_{g}_{h}", junk[:, 0:1])
                            if g == 9 and h == 5:
                                P.incw("s_v", v, v.tensor_copy(out=e2t[:, 0:128], in_=expE[:]))
                                P.cmark(v, "s_v", "eetap", junk[:, 0:1])
                            if g == 9 and h == 6:
                                P.wt(v, "eetapd")
                        P.wt(v, f"Y2_{g}_{gr}")
                        P.incw("s_v", v, v.tensor_tensor(out=e2t[:].rearrange("p (h d) -> p h d", d=64),
                                                     in0=pB[2][:].rearrange("p (h d) -> p h d", d=64),
                                                     in1=ecaT[:, gr * 8:(gr + 1) * 8].unsqueeze(2).broadcast_to((128, 8, 64)),
                                                     op=ALU.mult))
                        P.cmark(v, "s_v", f"e2used_{g}_{gr}", junk[:, 0:1])
                        P.wt(v, f"Y1_{g}_{gr * 8 + 7}")
                        P.incw("s_v", v, v.scalar_tensor_tensor(out=ysum[:], in0=pA[2][:], scalar=1.0,
                                                            in1=e2t[:], op0=ALU.mult, op1=ALU.add))
                        P.cmark(v, "s_v", f"ysumused_{g}_{gr}", junk[:, 0:1])
                        P.wt(v, "dvBev")
                        P.incw("s_v", v, v.tensor_tensor(out=skt[:].rearrange("p (h d) -> p h d", d=64),
                                                     in0=hstok[:, gr * 512:(gr + 1) * 512].rearrange("p (h d) -> p h d", d=64),
                                                     in1=dvB[:, gr * 8:(gr + 1) * 8].unsqueeze(2).broadcast_to((128, 8, 64)),
                                                     op=ALU.mult))
                        P.incw("s_v", v, v.tensor_tensor(out=ytok[:], in0=ysum[:], in1=skt[:], op=ALU.add))
                        P.cmark(v, "s_v", f"ytok_{g}_{gr}", junk[:, 0:1])
                        P.wt(v, f"ytT_{g}_{gr}")
                        if gr >= 1:
                            P.wt(v, f"ygnsp_{g}_{gr-1}")
                        P.incw("s_v", v, v.tensor_tensor(out=ytgnS[:],
                                                     in0=pA[3][:].bitcast(BF16)[:, 0:512],
                                                     in1=sgsl[:, gr * 512:(gr + 1) * 512], op=ALU.mult))
                        P.cmark(v, "s_v", f"ytgnused_{g}_{gr}", junk[:, 0:1])
                        P.incw("s_v", v, v.memset(junk[:, 5:6], 0.0))
                        P.cmark(v, "s_v", f"ygnf_{g}_{gr}", junk[:, 0:1])
                for r in range(4):
                    P.wt(v, f"dS_{g}_{r}")
                    off = r * 1024
                    P.incw("s_v", v, v.tensor_tensor(out=S[:, off:off + 1024].rearrange("p (h d) -> p h d", d=64),
                                                 in0=S[:, off:off + 1024].rearrange("p (h d) -> p h d", d=64),
                                                 in1=decB[:, r * 16:r * 16 + 16].unsqueeze(2).broadcast_to((128, 16, 64)),
                                                 op=ALU.mult))
                    P.incw("s_v", v, v.tensor_tensor(out=S[:, off:off + 512], in0=S[:, off:off + 512],
                                                 in1=pA[0][:], op=ALU.add))
                    P.incw("s_v", v, v.tensor_tensor(out=S[:, off + 512:off + 1024], in0=S[:, off + 512:off + 1024],
                                                 in1=pA[1][:], op=ALU.add))
                    P.cmark(v, "s_v", f"supd_{g}_{r}", junk[:, 0:1])
                if g == 7:
                    P.incw("s_v", v, v.tensor_scalar(out=S[:, 0:2048], in0=S[:, 0:2048], scalar1=mvS[:],
                                                 scalar2=None, op0=ALU.mult))
                    P.incw("s_v", v, v.tensor_scalar(out=S[:, 2048:4096], in0=S[:, 2048:4096], scalar1=mvS[:],
                                                 scalar2=None, op0=ALU.mult))
                if g >= 7:
                    if g >= 8:
                        P.wt(v, f"Y2_{g}_{NG-1}")
                    P.incw("s_v", v, v.tensor_copy(out=Sb[:], in_=S[:]))
                    P.cmark(v, "s_v", f"Sbready{g+1}", junk[:, 0:1])

        @blk.gpsimd
        def _(g):
            for g2 in range(8, 16):
                if g2 == 9:
                    P.wt(g, "captap")
                    P.inc("s_out", g.dma_start(out=caTO, in_=caT[:]), 16)
                    P.inc("s_out", g.dma_start(out=cavO, in_=cav[:]), 16)
                    P.inc("s_out", g.dma_start(out=dtsO, in_=dts[:]), 16)
                    P.mark("captapd", "s_out")
                    P.wt(g, "eetap")
                    P.inc("s_out", g.dma_start(out=eEO, in_=e2t[:, 0:128]), 16)
                    P.inc("s_out", g.dma_start(out=mddO, in_=mdd[:]), 16)
                    P.mark("eetapd", "s_out")
                for gr in range(8):
                    P.wt(g, f"ygnf_{g2}_{gr}")
                    for a in range(4):
                        P.inc("s_gp", g.dma_start(
                            out=ygnO[(gr * 4 + a) * 128:(gr * 4 + a + 1) * 128, (g2 - 8) * 128:(g2 - 7) * 128],
                            in_=ytgnS[:, a * 128:(a + 1) * 128]), 16)
                    P.mark(f"ygnsp_{g2}_{gr}", "s_gp")
                P.mark(f"ygnspall{g2}", "s_gp")
            g.wait_ge(P.SEM["s_gp"].h, P.SEM["s_gp"].n)
    return nc


def build_L3(marks, count_only):
    nc = bass.Bass("TRN2", target_bir_lowering=False)
    ygnI = nc.dram_tensor("ygnI", [4096, TM], BF16, kind="ExternalInput").ap()
    w_oT = nc.dram_tensor("w_oT", [4096, HID], F32, kind="ExternalInput").ap()
    nwI = nc.dram_tensor("nwm", [128, 32], F32, kind="ExternalInput").ap()
    sO = nc.dram_tensor("sO", [TM, HID], F32, kind="ExternalOutput").ap()

    P = Prog(marks, count_only)
    ctx = contextlib.ExitStack()
    with ctx:
        def sbuf(name, shape, dt=F32):
            return ctx.enter_context(nc.sbuf_tensor(name, shape, dt))

        ygn = [sbuf(f"ygn{i}", [128, TM], BF16) for i in range(32)]
        woTb = [sbuf(f"woTb{i}", [128, 1024], BF16) for i in range(32)]
        wbuf = [sbuf(f"wbuf{i}", [128, 1024]) for i in range(2)]
        sqscr = sbuf("sqscr", [128, TM])
        rstdR = sbuf("rstdR", [1, TM])
        rstdT = sbuf("rstdT", [128, 8])
        nwS = sbuf("nwS", [128, 32])
        onescf = sbuf("onescf", [128, 1])
        onesf1 = sbuf("onesf1", [1, 1])
        sval = sbuf("sval", [128, 1024])
        junk = sbuf("junk", [128, 8])
        pA = [ctx.enter_context(nc.psum_tensor(f"pA{i}", [128, 512], F32)) for i in range(4)]
        pB = [ctx.enter_context(nc.psum_tensor(f"pB{i}", [128, 512], F32)) for i in range(4)]
        P.setup(nc, ctx)
        blk = ctx.enter_context(nc.Block())

        @blk.sync
        def _(sync):
            P.inc("s_in", sync.dma_start(out=nwS[:], in_=nwI), 16)
            for ic in range(32):
                P.inc("s_in", sync.dma_start(out=ygn[ic][:], in_=ygnI[ic * 128:(ic + 1) * 128, :]), 16)
            P.mark("ygnload", "s_in")
            for ch in range(2):
                for ic in range(32):
                    t = ch * 32 + ic
                    if t >= 2:
                        P.wt(sync, f"wcast{t-2}")
                    P.inc("s_in", sync.dma_start(
                        out=wbuf[t % 2][:],
                        in_=w_oT.rearrange("(a p) c -> p a c", p=128)[:, ic, ch * 1024:(ch + 1) * 1024]), 16)
                    P.mark(f"wload{t}", "s_in")

        @blk.scalar
        def _(sc):
            for ic in range(32):
                P.wt(sc, "ygnload")
                if ic >= 1:
                    P.wt(sc, f"sqmm{ic-1}")
                P.incw("s_a", sc, sc.activation(out=sqscr[:], in_=ygn[ic][:], func=AF.Square))
                P.cmark(sc, "s_a", f"sq{ic}", junk[0:1, 0:1])
            P.wt(sc, "varrow")
            P.incw("s_a", sc, sc.activation(out=rstdR[:], in_=rstdR[:], func=AF.Ln))
            P.incw("s_a", sc, sc.activation(out=rstdR[:], in_=rstdR[:], func=AF.Exp, scale=-0.5))
            P.cmark(sc, "s_a", "rstd", junk[0:1, 0:1])
            for ch in range(2):
                for tt in range(8):
                    for nh2 in range(2):
                        t = ch * 16 + tt * 2 + nh2
                        P.wt(sc, f"opmm{t}")
                        P.wt(sc, "rstdT")
                        if nh2 == 0 and (ch, tt) != (0, 0):
                            pch, ptt = (ch, tt - 1) if tt >= 1 else (ch - 1, 7)
                            P.wt(sc, f"ssp_{pch}_{ptt}")
                        P.incw("s_a", sc, sc.activation(out=sval[:, nh2 * 512:(nh2 + 1) * 512],
                                                   in_=pA[t % 2][:], func=AF.Identity,
                                                   scale=rstdT[:, tt:tt + 1]))
                        P.cmark(sc, "s_a", f"opev{t}", junk[0:1, 0:1])
                        if nh2 == 1:
                            P.incw("s_a", sc, sc.memzero(junk[0:1, 4:5]))
                            P.cmark(sc, "s_a", f"svf_{ch}_{tt}", junk[0:1, 0:1])

        @blk.tensor
        def _(te):
            P.wt(te, "consts")
            for ic in range(32):
                P.wt(te, f"sq{ic}")
                ins = None
                for nh in range(2):
                    ins = te.matmul(pB[nh][0:1, :], onescf[:], sqscr[:, nh * 512:(nh + 1) * 512],
                                    start=(ic == 0), stop=(ic == 31))
                P.inc("s_pe", ins)
                P.mark(f"sqmm{ic}", "s_pe")
            P.mark("sqmmall", "s_pe")
            P.wt(te, "rstd")
            ins = None
            for tt in range(8):
                ins = te.matmul(pB[2][:, tt:tt + 1], rstdR[:, tt * 128:(tt + 1) * 128], onesf1[:],
                                start=(tt == 0), stop=(tt == 7))
            P.inc("s_pe", ins)
            P.mark("rstdmm", "s_pe")
            for ch in range(2):
                P.wt(te, f"wcastall{ch}")
                for tt in range(8):
                    for nh2 in range(2):
                        t = ch * 16 + tt * 2 + nh2
                        if t >= 2:
                            P.wt(te, f"opev{t-2}")
                        ins = None
                        for ic in range(32):
                            ins = te.matmul(pA[t % 2][:], ygn[ic][:, tt * 128:(tt + 1) * 128],
                                            woTb[ic][:, nh2 * 512:(nh2 + 1) * 512],
                                            start=(ic == 0), stop=(ic == 31))
                        P.inc("s_pe", ins)
                        P.mark(f"opmm{t}", "s_pe")

        @blk.vector
        def _(v):
            P.incw("s_v", v, v.memset(onescf[:], 1.0))
            P.incw("s_v", v, v.memset(onesf1[:], 1.0))
            P.cmark(v, "s_v", "consts", junk[:, 0:1])
            for ch in range(2):
                for ic in range(32):
                    t = ch * 32 + ic
                    P.wt(v, f"wload{t}")
                    if ch == 1 and ic == 0:
                        P.wt(v, "opmm15")
                    P.incw("s_v", v, v.tensor_scalar(out=woTb[ic][:], in0=wbuf[t % 2][:],
                                                 scalar1=nwS[:, ic:ic + 1], scalar2=None, op0=ALU.mult))
                    P.cmark(v, "s_v", f"wcast{t}", junk[:, 0:1])
                P.cmark(v, "s_v", f"wcastall{ch}", junk[:, 0:1])
                if ch == 0:
                    P.wt(v, "sqmmall")
                    P.incw("s_v", v, v.tensor_copy(out=rstdR[:, 0:512], in_=pB[0][0:1, :]))
                    P.incw("s_v", v, v.tensor_copy(out=rstdR[:, 512:1024], in_=pB[1][0:1, :]))
                    P.incw("s_v", v, v.tensor_scalar(out=rstdR[:], in0=rstdR[:], scalar1=1.0 / 4096.0,
                                                 scalar2=EPS, op0=ALU.mult, op1=ALU.add))
                    P.cmark(v, "s_v", "varrow", junk[:, 0:1])
                    P.wt(v, "rstdmm")
                    P.incw("s_v", v, v.tensor_copy(out=rstdT[:], in_=pB[2][:, 0:8]))
                    P.cmark(v, "s_v", "rstdT", junk[:, 0:1])

        @blk.gpsimd
        def _(g):
            for ch in range(2):
                for tt in range(8):
                    P.wt(g, f"svf_{ch}_{tt}")
                    P.inc("s_out", g.dma_start(out=sO[tt * 128:(tt + 1) * 128, ch * 1024:(ch + 1) * 1024],
                                               in_=sval[:]), 16)
                    P.mark(f"ssp_{ch}_{tt}", "s_out")
            g.wait_ge(P.SEM["s_out"].h, P.SEM["s_out"].n)
    return nc


def build_L4(marks, count_only, s4=4):
    nc = bass.Bass("TRN2", target_bir_lowering=False)
    qxnI = nc.dram_tensor("qxnI", [TM, HID], BF16, kind="ExternalInput").ap()
    WSRC = {"g": nc.dram_tensor("wg", [FFI, HID], F32, kind="ExternalInput").ap(),
            "u": nc.dram_tensor("wu", [FFI, HID], F32, kind="ExternalInput").ap(),
            "d": nc.dram_tensor("wd", [HID, FFI], F32, kind="ExternalInput").ap()}
    xmI = nc.dram_tensor("xm", [TM, HID], F32, kind="ExternalInput").ap()
    sI = nc.dram_tensor("sI", [TM, HID], F32, kind="ExternalInput").ap()
    outO = nc.dram_tensor("out", [TM, HID], F32, kind="ExternalOutput").ap()
    qwTO = nc.dram_tensor("qwTO", [128, 512], BF16, kind="ExternalOutput").ap()
    qiO = nc.dram_tensor("qiO", [128, 512], BF16, kind="ExternalOutput").ap()
    gateO = nc.dram_tensor("gateO", [128, 512], F32, kind="ExternalOutput").ap()

    qwD = {"g": nc.dram_tensor("qgD", [FFI, HID], BF16).ap(),
           "u": nc.dram_tensor("quD", [FFI, HID], BF16).ap(),
           "d": nc.dram_tensor("qdD", [HID, FFI], BF16).ap()}
    qiD = nc.dram_tensor("qiD", [TM, FFI], BF16).ap()

    WT = ([("g", r, 0) for r in range(0, FFI, 128)] +
          [("u", r, 0) for r in range(0, FFI, 128)] +
          [("d", r, cp) for r in range(0, HID, 128) for cp in range(4)])
    NWT = len(WT)

    P = Prog(marks, count_only)
    ctx = contextlib.ExitStack()
    with ctx:
        def sbuf(name, shape, dt=F32):
            return ctx.enter_context(nc.sbuf_tensor(name, shape, dt))

        qxnT = [sbuf(f"qxnT{i}", [128, TM], BF16) for i in range(16)]
        qwT = [sbuf(f"qwT{i}", [128, 512], BF16) for i in range(32)]
        wq_in = [sbuf(f"wqin{i}", [128, HID]) for i in range(2)]
        q_o = sbuf("q_o", [128, HID], BF16)
        i_q = sbuf("i_q", [128, 512], BF16)
        inters = sbuf("inters", [128, 512])
        sg8 = sbuf("sg8", [128, 512], F32)
        gval = sbuf("gval", [128, HID])
        svalE = sbuf("svalE", [128, HID])
        xt2 = [sbuf(f"xt2{i}", [128, HID]) for i in range(2)]
        qiTb = [sbuf(f"qiTb{i}", [128, 256], BF16) for i in range(2)]
        qdTb = [sbuf(f"qdTb{i}", [128, HID], BF16) for i in range(2)]
        junk = sbuf("junk", [128, 8])
        QB = qbufs(sbuf, HID)
        pA = [ctx.enter_context(nc.psum_tensor(f"pA{i}", [128, 512], F32)) for i in range(4)]
        pB = [ctx.enter_context(nc.psum_tensor(f"pB{i}", [128, 512], F32)) for i in range(4)]
        P.setup(nc, ctx)
        blk = ctx.enter_context(nc.Block())

        @blk.scalar
        def _(sc):
            for t in range(NWT):
                wk, r0, cp = WT[t]
                if t >= 2:
                    P.wt(sc, f"wq_q_{t-2}")
                src = WSRC[wk][r0:r0 + 128, :] if wk != "d" else WSRC[wk][r0:r0 + 128, cp * 2048:(cp + 1) * 2048]
                P.inc("s_sp", sc.dma_start(out=wq_in[t % 2][:], in_=src), 16)
                P.mark(f"wq_ld_{t}", "s_sp")
            if s4 < 2:
                return
            for j in range(16):
                for tt in range(8):
                    P.wt(sc, f"gu_{j}_{tt}")
                    if j > 0 or tt > 0:
                        pj, ptt = (j, tt - 1) if tt >= 1 else (j - 1, 7)
                        P.wt(sc, f"int_{pj}_{ptt}")
                    P.incw("s_a", sc, sc.activation(out=sg8[:], in_=pA[2 * (tt % 2)][:], func=AF.Silu))
                    P.cmark(sc, "s_a", f"silu_{j}_{tt}", junk[0:1, 0:1])
            if s4 < 3:
                return
            for tg in range(4):
                for t2 in range(2):
                    P.wt(sc, f"dmmlast_{tg}")
                    tt = tg * 2 + t2
                    if s4 >= 4 and tt >= 1:
                        P.wt(sc, f"odone{tt-1}")
                    for c2 in range(4):
                        P.incw("s_a", sc, sc.activation(out=gval[:, c2 * 512:(c2 + 1) * 512],
                                                   in_=(pA if t2 == 0 else pB)[c2][:], func=AF.Silu))
                    P.cmark(sc, "s_a", f"gval_{tg}_{t2}", junk[0:1, 0:1])

        @blk.vector
        def _(v):
            for t in range(NWT):
                wk, r0, cp = WT[t]
                P.wt(v, f"wq_ld_{t}")
                if t >= 1:
                    P.wt(v, f"wq_sp_{t-1}")
                emit_qw_m1(P, v, QB, wq_in[t % 2][:], q_o[:], HID)
                P.incw("s_v", v, v.memset(junk[:, 1:2], 0.0))
                P.cmark(v, "s_v", f"wq_f_{t}", junk[:, 0:1])
                P.cmark(v, "s_v", f"wq_q_{t}", junk[:, 0:1])
            if s4 < 2:
                P.mark("l4done", "s_v")
                return
            for j in range(16):
                for tt in range(8):
                    P.wt(v, f"silu_{j}_{tt}")
                    if j > 0 or tt > 0:
                        pk = (j, tt - 1) if tt >= 1 else (j - 1, 7)
                        P.wt(v, f"qisp_{pk[0]}_{pk[1]}")
                    P.incw("s_v", v, v.tensor_tensor(out=inters[:], in0=sg8[:],
                                                 in1=pA[2 * (tt % 2) + 1][:], op=ALU.mult))
                    P.cmark(v, "s_v", f"int_{j}_{tt}", junk[:, 0:1])
                    emit_qact_m3(P, v, QB, inters[:], i_q[:], 512)
                    P.incw("s_v", v, v.memset(junk[:, 6:7], 0.0))
                    P.cmark(v, "s_v", f"qif_{j}_{tt}", junk[:, 0:1])
            if s4 < 4:
                P.mark("l4done", "s_v")
                if s4 >= 3:
                    for tg in range(4):
                        for t2 in range(2):
                            P.wt(v, f"gval_{tg}_{t2}")
                    P.cmark(v, "s_v", "l4done", junk[:, 0:1])
                return
            for tg in range(4):
                for t2 in range(2):
                    tt = tg * 2 + t2
                    P.wt(v, f"gval_{tg}_{t2}")
                    P.wt(v, f"xs{tt}")
                    P.incw("s_v", v, v.tensor_tensor(out=gval[:], in0=gval[:], in1=svalE[:], op=ALU.mult))
                    P.incw("s_v", v, v.tensor_tensor(out=gval[:], in0=gval[:], in1=xt2[tt % 2][:], op=ALU.add))
                    P.incw("s_v", v, v.memset(junk[:, 7:8], 0.0))
                    P.cmark(v, "s_v", f"ep_{tt}", junk[:, 0:1])

        @blk.gpsimd
        def _(g):
            for t in range(NWT):
                wk, r0, cp = WT[t]
                P.wt(g, f"wq_f_{t}")
                dst = qwD[wk][r0:r0 + 128, :] if wk != "d" else qwD[wk][r0:r0 + 128, cp * 2048:(cp + 1) * 2048]
                P.inc("s_gp", g.dma_start(out=dst, in_=q_o[:]), 16)
                P.mark(f"wq_sp_{t}", "s_gp")
            P.mark("wq_spall", "s_gp")
            if s4 >= 2:
                P.wt(g, "w0")
                P.inc("s_out", g.dma_start(out=qwTO, in_=qwT[0][:]), 16)
                P.mark("qwtapd", "s_out")
                P.wt(g, "int_0_0")
                P.inc("s_out", g.dma_start(out=gateO, in_=inters[:]), 16)
                P.wt(g, "qif_0_0")
                P.inc("s_out", g.dma_start(out=qiO, in_=i_q[:]), 16)
                P.mark("qitapd", "s_out")
                for j in range(16):
                    for tt in range(8):
                        P.wt(g, f"qif_{j}_{tt}")
                        P.inc("s_gp", g.dma_start(out=qiD[tt * 128:(tt + 1) * 128, j * 512:(j + 1) * 512],
                                                  in_=i_q[:]), 16)
                        P.mark(f"qisp_{j}_{tt}", "s_gp")
                P.mark("qspall", "s_gp")
            if s4 >= 4:
                for tt in range(8):
                    P.wt(g, f"ep_{tt}")
                    P.inc("s_out", g.dma_start(out=outO[tt * 128:(tt + 1) * 128, :], in_=gval[:]), 16)
                    P.mark(f"odone{tt}", "s_out")
            else:
                P.wt(g, "l4done")
                P.inc("s_out", g.dma_start(out=outO[0:128, :], in_=gval[:]), 16)
            g.wait_ge(P.SEM["s_out"].h, P.SEM["s_out"].n)
            g.wait_ge(P.SEM["s_gp"].h, P.SEM["s_gp"].n)

        @blk.sync
        def _(sync):
            for k in range(16):
                P.inc("s_in", sync.dma_start_transpose(out=qxnT[k][:],
                                                       in_=qxnI[:, k * 128:(k + 1) * 128]), 16)
            P.mark("qxnT", "s_in")
            if s4 < 2:
                return
            for j in range(16):
                gu_done = (j * 512 + 512) // 512 - 1
                P.wt(sync, f"wq_sp_{64 + 4 * j + 3}")
                if j == 1:
                    P.wt(sync, "qwtapd")
                if j >= 1:
                    P.wt(sync, f"ttdone{j-1}")
                for wi, wk in enumerate(("g", "u")):
                    for k in range(16):
                        P.inc("s_in", sync.dma_start_transpose(
                            out=qwT[wi * 16 + k][:],
                            in_=qwD[wk][j * 512:(j + 1) * 512, k * 128:(k + 1) * 128]), 16)
                P.mark(f"w{j}", "s_in")
            if s4 < 3:
                return
            P.wt(sync, "wq_spall")
            P.wt(sync, "qspall")
            for tg in range(4):
                if s4 >= 4:
                    t0 = 2 * tg
                    if t0 >= 1:
                        P.wt(sync, f"ep_{t0-1}")
                    P.inc("s_in", sync.dma_start(out=svalE[:], in_=sI[t0 * 128:(t0 + 1) * 128, :]), 16)
                    P.inc("s_in", sync.dma_start(out=xt2[t0 % 2][:], in_=xmI[t0 * 128:(t0 + 1) * 128, :]), 16)
                    P.mark(f"xs{t0}", "s_in")
                for k in range(64):
                    if tg > 0 or k >= 2:
                        pk = (tg, k - 2) if k >= 2 else (tg - 1, 62 + k)
                        P.wt(sync, f"dmm_{pk[0]}_{pk[1]}")
                    P.inc("s_in", sync.dma_start_transpose(
                        out=qiTb[k % 2][:],
                        in_=qiD[tg * 256:(tg + 1) * 256, k * 128:(k + 1) * 128]), 16)
                    P.inc("s_in", sync.dma_start_transpose(
                        out=qdTb[k % 2][:, 0:1024],
                        in_=qwD["d"][0:1024, k * 128:(k + 1) * 128]), 16)
                    P.inc("s_in", sync.dma_start_transpose(
                        out=qdTb[k % 2][:, 1024:2048],
                        in_=qwD["d"][1024:2048, k * 128:(k + 1) * 128]), 16)
                    P.mark(f"dk_{tg}_{k}", "s_in")
                if s4 >= 4:
                    t1 = 2 * tg + 1
                    P.wt(sync, f"ep_{t1-1}")
                    P.inc("s_in", sync.dma_start(out=svalE[:], in_=sI[t1 * 128:(t1 + 1) * 128, :]), 16)
                    P.inc("s_in", sync.dma_start(out=xt2[t1 % 2][:], in_=xmI[t1 * 128:(t1 + 1) * 128, :]), 16)
                    P.mark(f"xs{t1}", "s_in")

        @blk.tensor
        def _(te):
            if s4 < 2:
                return
            P.wt(te, "qxnT")
            for j in range(16):
                P.wt(te, f"w{j}")
                for tt in range(8):
                    if j > 0 or tt >= 2:
                        pk = (j, tt - 2) if tt >= 2 else (j - 1, 6 + tt)
                        P.wt(te, f"int_{pk[0]}_{pk[1]}")
                    for wi in range(2):
                        ins = None
                        for k in range(16):
                            ins = te.matmul(pA[2 * (tt % 2) + wi][:],
                                            qxnT[k][:, tt * 128:(tt + 1) * 128],
                                            qwT[wi * 16 + k][:],
                                            start=(k == 0), stop=(k == 15))
                        P.inc("s_pe", ins)
                    P.mark(f"gu_{j}_{tt}", "s_pe")
                P.mark(f"ttdone{j}", "s_pe")
            if s4 < 3:
                return
            for tg in range(4):
                for k in range(64):
                    P.wt(te, f"dk_{tg}_{k}")
                    if k == 0 and tg == 0:
                        P.wt(te, "int_15_7")
                    if k == 0 and tg >= 1:
                        P.wt(te, f"gval_{tg-1}_1")
                    ins = None
                    for t2 in range(2):
                        for c2 in range(4):
                            bank = (pA if t2 == 0 else pB)[c2]
                            ins = te.matmul(bank[:],
                                            qiTb[k % 2][:, t2 * 128:(t2 + 1) * 128],
                                            qdTb[k % 2][:, c2 * 512:(c2 + 1) * 512],
                                            start=(k == 0), stop=(k == 63))
                    P.inc("s_pe", ins)
                    P.mark(f"dmm_{tg}_{k}", "s_pe")
                P.mark(f"dmmlast_{tg}", "s_pe")
    return nc


def kernel(L0_attn_hidden_states, L0_attn_in_proj_w, L0_attn_conv_w, L0_attn_conv_b,
           L0_attn_dt_bias, L0_attn_A_log, L0_attn_D, L0_attn_norm_weight,
           L0_attn_out_proj_w, L0_ffn_w_gate, L0_ffn_w_up, L0_ffn_w_down):
    x = np.ascontiguousarray(np.asarray(L0_attn_hidden_states, dtype=np.float32))
    w_in = np.ascontiguousarray(np.asarray(L0_attn_in_proj_w, dtype=np.float32))
    conv_w = np.asarray(L0_attn_conv_w, dtype=np.float32)
    conv_b = np.asarray(L0_attn_conv_b, dtype=np.float32)
    dtb = np.asarray(L0_attn_dt_bias, dtype=np.float32)
    alog = np.asarray(L0_attn_A_log, dtype=np.float32)
    dv = np.asarray(L0_attn_D, dtype=np.float32)
    nw = np.asarray(L0_attn_norm_weight, dtype=np.float32)
    w_o = np.ascontiguousarray(np.asarray(L0_attn_out_proj_w, dtype=np.float32))
    wg = np.ascontiguousarray(np.asarray(L0_ffn_w_gate, dtype=np.float32))
    wu = np.ascontiguousarray(np.asarray(L0_ffn_w_up, dtype=np.float32))
    wd = np.ascontiguousarray(np.asarray(L0_ffn_w_down, dtype=np.float32))

    w_inT = np.ascontiguousarray(w_in.T)
    w_oT = np.ascontiguousarray(w_o.T)
    cw = conv_w[:, 0, :]
    convdiag = np.zeros((136 * 128, 128), np.float32)
    for k in range(4):
        for cc in range(34):
            blkd = convdiag[(k * 34 + cc) * 128:(k * 34 + cc + 1) * 128, :]
            np.fill_diagonal(blkd, cw[cc * 128:(cc + 1) * 128, k])
    convbM = np.ascontiguousarray(conv_b.reshape(34, 128).T)
    dtbM = dtb.reshape(NH, 1)
    alogM = alog.reshape(NH, 1)
    dvM = dv.reshape(1, NH)
    nwM = np.ascontiguousarray(nw.reshape(32, 128).T)
    identb = np.eye(128, dtype=np.float32).astype(ml_dtypes.bfloat16)
    identf = np.eye(128, dtype=np.float32)
    utneg = np.tril(np.full((128, 128), -1e30, np.float32), k=-1)

    zeros_x = np.zeros((TM, HID), np.float32)

    in1, in2, in3, in4 = [], [], [], []
    for c in range(NCORE):
        b, s = c // 2, c % 2
        xm = np.ascontiguousarray(x[b, s * TM:(s + 1) * TM, :])
        xr = np.ascontiguousarray(x[b, 0:TM, :]) if s == 1 else zeros_x
        mv = np.full((128, 1), float(s), np.float32)
        in1.append(dict(xm=xm, xr=xr, w_inT=w_inT, convdiag=convdiag, convb=convbM,
                        identb=identb))
        in2.append(dict(dtb=dtbM, alog=alogM, dvec=dvM, identb=identb, identf=identf,
                        utneg=utneg, mvec=mv))
        in3.append(dict(w_oT=w_oT, nwm=nwM))
        in4.append(dict(wg=wg, wu=wu, wd=wd, xm=xm))

    cores = list(range(NCORE))
    nc1 = _mk(build_L1)
    r1 = run_bass_kernel_spmd(nc1, in1, core_ids=cores)
    for c in range(NCORE):
        in2[c].update(hsI=r1.results[c]["hsO"], BfI=r1.results[c]["BfO"],
                      CfI=r1.results[c]["CfO"], dtI=r1.results[c]["dtO"],
                      sgI=r1.results[c]["sgO"])
    nc2 = _mk(build_L2)
    r2 = run_bass_kernel_spmd(nc2, in2, core_ids=cores)
    for c in range(NCORE):
        in3[c].update(ygnI=r2.results[c]["ygnO"])
    nc3 = _mk(build_L3)
    r3 = run_bass_kernel_spmd(nc3, in3, core_ids=cores)
    for c in range(NCORE):
        in4[c].update(qxnI=r1.results[c]["qxnO"], sI=r3.results[c]["sO"])
    nc4 = _mk(build_L4)
    r4 = run_bass_kernel_spmd(nc4, in4, core_ids=cores)

    out = np.zeros((4, 2048, HID), np.float32)
    for c in range(NCORE):
        b, s = c // 2, c % 2
        out[b, s * TM:(s + 1) * TM, :] = np.asarray(r4.results[c]["out"])
    return out
